# version 36
# baseline (speedup 1.0000x reference)
import numpy as np
import concourse.bass as bass
import concourse.mybir as mybir
from concourse.bass_utils import run_bass_kernel_spmd

F32 = mybir.dt.float32
BF16 = mybir.dt.bfloat16
I32 = mybir.dt.int32
AF = mybir.ActivationFunctionType
ALU = mybir.AluOpType
AX = mybir.AxisListType

P = 128
D = 1024
HID = 4096
NST = 16
NCORES = 8
T = 2048
NTILE = T // P
NPAIR = NTILE // 2
NGRP = NTILE // 2
KD = D // P
KH = HID // P
CMAG = 12582912.0

ENGINES = ("sync", "scalar", "vector", "tensor", "gpsimd")


class Plan:
    def __init__(self):
        self.ops = {e: [] for e in ENGINES}
        self.cnt = {}
        self.waited = {}

    def sem(self, name):
        self.cnt[name] = 0

    def op(self, eng, build, waits=(), sem=None, inc=1):
        for w in waits:
            if w is None:
                continue
            s, v = w
            key = (eng, s)
            if self.waited.get(key, -1) < v:
                self.ops[eng].append(lambda e, S, s=s, v=v: e.wait_ge(S[s], v))
                self.waited[key] = v
        if sem is not None:
            self.cnt[sem] += inc
            v2 = self.cnt[sem]
            self.ops[eng].append(lambda e, S, b=build, s=sem, n=inc: b(e).then_inc(S[s], n))
            return (sem, v2)
        self.ops[eng].append(lambda e, S, b=build: b(e))
        return None

    def raw(self, eng, fn):
        self.ops[eng].append(fn)


def build_nc():
    nc = bass.Bass("TRN2", target_bir_lowering=False, debug=False)

    x = nc.dram_tensor("x", [T, D], F32, kind="ExternalInput").ap()
    W1 = nc.dram_tensor("W1", [D, D], F32, kind="ExternalInput").ap()
    b1 = nc.dram_tensor("b1", [1, D], F32, kind="ExternalInput").ap()
    W2 = nc.dram_tensor("W2", [NST, D], F32, kind="ExternalInput").ap()
    b2 = nc.dram_tensor("b2", [1, NST], F32, kind="ExternalInput").ap()
    W3 = nc.dram_tensor("W3", [NST, D], F32, kind="ExternalInput").ap()
    b3 = nc.dram_tensor("b3", [1, NST], F32, kind="ExternalInput").ap()
    fw1 = nc.dram_tensor("fw1", [HID, D], F32, kind="ExternalInput").ap()
    fb1 = nc.dram_tensor("fb1", [HID, 1], F32, kind="ExternalInput").ap()
    fw2 = nc.dram_tensor("fw2", [D, HID], F32, kind="ExternalInput").ap()
    fb2 = nc.dram_tensor("fb2", [1, D], F32, kind="ExternalInput").ap()

    y = nc.dram_tensor("y", [T, D], F32, kind="ExternalOutput").ap()

    s_xnb = nc.dram_tensor("s_xnb", [T, D], BF16, kind="Internal").ap()
    s_xq = nc.dram_tensor("s_xq", [T, D], BF16, kind="Internal").ap()
    s_attn = nc.dram_tensor("s_attn", [T, D], BF16, kind="Internal").ap()

    SBUF_TOP = 229376 - 32
    BASE = 16384 + 1024
    cur = [BASE]
    off = {}

    def at(name, shape, dtype, offset=None):
        o = cur[0] if offset is None else offset
        h = nc.alloc_sbuf_tensor_at(name, shape, dtype, offset=o)
        if offset is None:
            eb = {F32: 4, BF16: 2, I32: 4}[dtype]
            n = 1
            for s in shape[1:]:
                n *= s
            off[name] = o
            cur[0] = o + ((n * eb + 31) // 32) * 32
        return h.ap()

    W23T = at("W23T", [P, KD * 32], BF16)
    ident = at("ident", [P, P], F32)
    rows_a = at("rows_a", [P, D], F32)
    b1row = rows_a[0:1, :]
    b2qrow = rows_a[32:33, :]
    fb2row = rows_a[64:65, :]
    rows_b = at("rows_b", [P, 160], F32)
    onesr = rows_b[0:1, 0:128]
    ones32 = rows_b[32:33, 0:128]
    ones64 = rows_b[64:65, 0:128]
    b23row = rows_b[64:65, 128:160]
    b1q16 = at("b1q16", [P, KH], F32)
    b1qi = at("b1qi", [P, KH], I32)
    fb1c = at("fb1c", [P, KH], F32)
    cC = at("cC", [P, 1], F32)
    c64 = at("c64", [P, 1], F32)
    c16 = at("c16", [P, 1], F32)
    ceps = at("ceps", [P, 1], F32)
    cmhalf = at("cmhalf", [P, 1], F32)
    c164 = at("c164", [P, 1], F32)
    bn12 = [at("bn12_%d" % i, [P, 12], F32) for i in range(2)]
    bn2 = [at("bn2_%d" % i, [P, 2], F32) for i in range(2)]
    lnv = [at("lnv_%d" % i, [P, 1], F32) for i in range(2)]
    rstd = [at("rstd_%d" % i, [P, 1], F32) for i in range(2)]
    fbst = at("fbst", [32, P], F32)
    bcv = at("bcv", [P, 1], F32)
    evbc = at("evbc", [P, 32], F32)
    prodbc = at("prodbc", [P, 16], F32)

    xin = [at("xin%d" % i, [P, D], F32) for i in range(2)]
    xn_tm = [at("xn%d" % i, [P, D], F32) for i in range(4)]
    NRING = 4
    xnb = at("xnb", [P, D], BF16)
    xqb = at("xqb", [P, D], BF16)
    qt = at("qt", [P, 512], F32)
    qi = at("qi", [P, 512], I32)
    xn_fm = [at("xnfm%d" % i, [P, KD * 256], BF16) for i in range(2)]
    sp_u = [at("sp_u%d" % i, [P, 512], F32) for i in range(2)]
    sp_v = [at("sp_v%d" % i, [P, 512], F32) for i in range(2)]
    attnb = at("attnb", [P, D], BF16)
    wld = [at("wld%d" % i, [P, D], F32) for i in range(2)]
    wt1 = [at("wt1_%d" % i, [P, D], F32) for i in range(3)]
    wi1 = [at("wi1_%d" % i, [P, D], I32, offset=off["wt1_%d" % i]) for i in range(3)]

    xq_fm = [at("xqfm%d" % i, [P, KD * 256], BF16, offset=off["xin%d" % i]) for i in range(2)]
    hq = [at("hq%d" % i, [P, 256], BF16, offset=off["xnb"] + i * 512) for i in range(2)]
    epi_t0 = at("epit", [P, 256], F32, offset=off["xnb"] + 1024)
    epi_t = [epi_t0, epi_t0]
    epi_i = at("epii", [P, 256], I32, offset=off["xqb"])
    xre = at("xre", [P, D], F32, offset=off["xn0"])
    atre = at("atre", [P, D], BF16, offset=off["xn1"])
    t1b = [at("t1b%d" % i, [P, 512], F32, offset=off["xn2"] + i * 2048) for i in range(2)]

    WR = SBUF_TOP - 131072
    assert cur[0] <= WR, f"main region overflows into weight region: {cur[0]} > {WR}"
    w1qT = at("w1qT", [P, KD * HID], BF16, offset=WR)
    w1qT3 = w1qT.rearrange("p (k h) -> p k h", k=KD)
    w2qT = at("w2qT", [P, KH * D], BF16, offset=WR + 65536)
    w2qT3 = w2qT.rearrange("p (k h) -> p k h", k=KH)
    W1T = at("W1T", [P, KD * D], BF16, offset=WR + 65536 + 49152)
    W1T3 = W1T.rearrange("p (k h) -> p k h", k=KD)

    import contextlib
    stack = contextlib.ExitStack()
    pb = [stack.enter_context(nc.psum_tensor("pb%d" % i, [P, 512], F32)) for i in range(8)]

    pl = Plan()
    SEMS = ("dsm", "dwl0", "dwl1", "dx0", "dx1", "txn0", "txn1", "txq0", "txq1",
            "dwbx", "dwbq", "dwba", "drx", "dra", "dst0", "dst1", "cv", "cs", "pe", "cg")
    for s in SEMS:
        pl.sem(s)

    for ap_, val in ((cC, CMAG), (c64, 64.0), (c16, 16.0), (ceps, 1e-5),
                     (cmhalf, -0.5), (c164, 1.0 / 64.0)):
        nc.const_aps.register(F32, val, ap_[:, :])
        pl.op("vector", lambda e, a=ap_, v=val: e.memset(a[:, :], v))
    pl.op("vector", lambda e: e.memset(rows_b[:, 0:128], 1.0))
    consts_tok = pl.op("vector", lambda e: e.memset(b23row[:, :], 0.0), sem="cv")
    cg_tok = pl.op("gpsimd", lambda e: e.memset(ident[:, :], 0.0), sem="cg")
    cg_tok = pl.op("gpsimd", lambda e: e.affine_select(
        out=ident[:, :], in_=ident[:, :], compare_op=ALU.not_equal, fill=1.0,
        base=0, pattern=[[-1, P]], channel_multiplier=1),
        waits=(cg_tok,), sem="cg")

    pl.op("sync", lambda e: e.dma_start(out=b1row[:, :], in_=b1[:, :]), sem="dsm", inc=16)
    pl.op("sync", lambda e: e.dma_start(out=b23row[:, 0:NST], in_=b2[:, :]),
          waits=(consts_tok,), sem="dsm", inc=16)
    pl.op("sync", lambda e: e.dma_start(out=b23row[:, NST:2 * NST], in_=b3[:, :]),
          sem="dsm", inc=16)
    pl.op("sync", lambda e: e.dma_start(out=fb2row[:, :], in_=fb2[:, :]), sem="dsm", inc=16)
    fb1_ld = pl.op("sync", lambda e: e.dma_start(
        out=fbst[:, :], in_=fb1.rearrange("(a b) one -> a (b one)", a=KH)),
        sem="dsm", inc=16)
    dsm_all = ("dsm", 16 * 5)
    fb1_tp = pl.op("tensor", lambda e: e.transpose(
        pb[6][:, 0:KH], fbst[:, :], ident[0:32, 0:32]),
        waits=(fb1_ld, cg_tok), sem="pe")
    fb1_tok = pl.op("scalar", lambda e: e.activation(
        fb1c[:, :], pb[6][:, 0:KH], AF.Copy), waits=(fb1_tp,), sem="cs")

    t = pl.op("vector", lambda e: e.tensor_scalar(b2qrow[:, :], fb2row[:, :], 64.0, CMAG, ALU.mult, ALU.add),
              waits=(dsm_all, consts_tok), sem="cv")
    t = pl.op("vector", lambda e: e.tensor_scalar(wi1[0][0:1, 0:D], b2qrow[:, :], -CMAG, 127.0, ALU.add, ALU.min),
              waits=(t,), sem="cv")
    b2q_tok = pl.op("vector", lambda e: e.tensor_scalar(b2qrow[:, :], wi1[0][0:1, 0:D], -128.0, 1.0 / 64.0, ALU.max, ALU.mult),
                    waits=(t,), sem="cv")
    t = pl.op("vector", lambda e: e.tensor_scalar(b1q16[:, :], fb1c[:, :], 64.0, CMAG, ALU.mult, ALU.add),
              waits=(fb1_tok,), sem="cv")
    t = pl.op("vector", lambda e: e.tensor_scalar(b1qi[:, :], b1q16[:, :], -CMAG, 127.0, ALU.add, ALU.min),
              waits=(t,), sem="cv")
    t = pl.op("vector", lambda e: e.tensor_scalar(b1q16[:, :], b1qi[:, :], -128.0, 1.0 / 64.0, ALU.max, ALU.mult),
              waits=(t,), sem="cv")
    b1q_tok = pl.op("vector", lambda e: e.tensor_scalar_mul(b1q16[:, :], b1q16[:, :], 16.0),
                    waits=(t,), sem="cv")

    wq_jobs = [("w1c", i) for i in range(KD)] + [("w23",)] + \
              [("w1", i) for i in range(KH)] + \
              [("w2", i, h) for h in range(3) for i in range(KD)]
    wq_defer = [("w2", i, 3) for i in range(KD)]
    all_jobs = wq_jobs + wq_defer

    wld_free = [None, None]
    wt1_free = [b2q_tok, None, None]
    pbank67_free = [fb1_tok, None]
    lt_toks = [None] * len(all_jobs)
    w1col = [None] * KH
    w2grp = [None] * 4
    W1T_ready = [None]
    W23T_ready = [None]
    last_delta = [None]

    def wq_load(n):
        job = all_jobs[n]
        bi = n % 2
        if job[0] == "w1c":
            i = job[1]
            lt_toks[n] = pl.op("scalar", lambda e, i=i, bi=bi: e.dma_start(
                out=wld[bi][:, :], in_=W1[i * P:(i + 1) * P, :]),
                waits=(wld_free[bi],), sem="dwl%d" % bi, inc=16)
        elif job[0] == "w23":
            pl.op("scalar", lambda e, bi=bi: e.dma_start(
                out=wld[bi][0:NST, :], in_=W2[:, :]),
                waits=(wld_free[bi],), sem="dwl%d" % bi, inc=16)
            lt_toks[n] = pl.op("scalar", lambda e, bi=bi: e.dma_start(
                out=wld[bi][NST:2 * NST, :], in_=W3[:, :]), sem="dwl%d" % bi, inc=16)
        elif job[0] == "w1":
            i = job[1]
            lt_toks[n] = pl.op("scalar", lambda e, i=i, bi=bi: e.dma_start(
                out=wld[bi][:, :], in_=fw1[i * P:(i + 1) * P, :]),
                waits=(wld_free[bi],), sem="dwl%d" % bi, inc=16)
        else:
            i, h = job[1], job[2]
            lt_toks[n] = pl.op("scalar", lambda e, i=i, h=h, bi=bi: e.dma_start(
                out=wld[bi][:, :], in_=fw2[i * P:(i + 1) * P, h * D:(h + 1) * D]),
                waits=(wld_free[bi],), sem="dwl%d" % bi, inc=16)

    def wq_job(n):
        job = all_jobs[n]
        bi = n % 2
        if lt_toks[n] is None:
            wq_load(n)
        lt = lt_toks[n]
        kind = job[0]
        if kind == "w23":
            pend_T.append((job, kind, wld[bi], lt, 0, None, 0, bi, 0))
            if len(pend_T) > 1:
                wq_back(pend_T.pop(0))
            if n + 1 < len(all_jobs) and lt_toks[n + 1] is None:
                wq_load(n + 1)
            return
        b3 = n % 3
        if kind == "w1c":
            src = wld[bi]
            ready = lt
            i = job[1]
            dst3, c_base = W1T3, 0
        else:
            p1 = pl.op("scalar", lambda e, bi=bi, b3=b3: e.activation(
                wt1[b3][:, :], wld[bi][:, :], AF.Copy, bias=CMAG, scale=64.0),
                waits=(lt, consts_tok, wt1_free[b3]), sem="cs")
            wld_free[bi] = p1
            p2 = pl.op("vector", lambda e, b3=b3: e.tensor_scalar(
                wi1[b3][:, :], wt1[b3][:, :], -CMAG, 127.0, ALU.add, ALU.min),
                waits=(p1,), sem="cv")
            p3 = pl.op("vector", lambda e, b3=b3: e.tensor_scalar(
                wt1[b3][:, :], wi1[b3][:, :], -128.0, 1.0, ALU.max, ALU.mult),
                waits=(p2,), sem="cv")
            src = wt1[b3]
            ready = p3
            i = job[1]
            if kind == "w1":
                dst3, c_base = w1qT3, 0
            else:
                dst3, c_base = w2qT3, 8 * job[2]
        pend_T.append((job, kind, src, ready, i, dst3, c_base, bi, b3))
        if len(pend_T) > 1:
            wq_back(pend_T.pop(0))
        if n + 1 < len(all_jobs) and lt_toks[n + 1] is None:
            wq_load(n + 1)

    def wq_back(ctx):
        job, kind, src, ready, i, dst3, c_base, bi, b3 = ctx
        if kind == "w23":
            tk = None
            for c in range(KD):
                tk = pl.op("tensor", lambda e, c=c, src=src: e.transpose(
                    pb[6][:, c * 32:(c + 1) * 32], src[0:32, c * P:(c + 1) * P],
                    ident[0:32, 0:32]),
                    waits=(ready, cg_tok, pbank67_free[0]) if c == 0 else (),
                    sem="pe" if c == KD - 1 else None)
            cp = pl.op("scalar", lambda e: e.activation(
                W23T[:, :], pb[6][:, 0:KD * 32], AF.Copy),
                waits=(tk,), sem="cs")
            pbank67_free[0] = cp
            wld_free[bi] = tk
            W23T_ready[0] = cp
            pend_lim[0] = 2
            return
        h3 = kind == "w2" and job[2] == 3
        cps = []
        tkl = None
        for w in range(2):
            for c4 in range(4):
                c = 4 * w + c4
                tkl = pl.op("tensor", lambda e, c=c, c4=c4, w=w, src=src: e.transpose(
                    pb[6 + w][:, c4 * 128:(c4 + 1) * 128], src[:, c * P:(c + 1) * P],
                    ident[:, :]),
                    waits=((ready, cg_tok, pbank67_free[w]) if c4 == 0 else ()),
                    sem="pe" if c4 == 3 else None)
            dd = dst3[:, c_base + 4 * w:c_base + 4 * w + 4, i * P:(i + 1) * P]
            xw = (tkl,) + ((last_delta[0],) if h3 else ())
            if w == 0:
                if kind == "w1c":
                    cp = pl.op("scalar", lambda e, dd=dd: e.activation(
                        dd, pb[6][:, :], AF.Copy),
                        waits=xw, sem="cs")
                else:
                    cp = pl.op("scalar", lambda e, dd=dd: e.activation(
                        dd, pb[6][:, :], AF.Copy, scale=c164[:, :]),
                        waits=xw + (consts_tok,), sem="cs")
            else:
                if kind == "w1c":
                    cp = pl.op("vector", lambda e, dd=dd: e.tensor_copy(dd, pb[7][:, :]),
                               waits=xw, sem="cv")
                else:
                    cp = pl.op("vector", lambda e, dd=dd: e.tensor_scalar_mul(dd, pb[7][:, :], 1.0 / 64.0),
                               waits=xw, sem="cv")
            pbank67_free[w] = cp
            cps.append(cp)
        if kind == "w1c":
            wld_free[bi] = tkl
            W1T_ready[0] = tuple(cps)
        else:
            wt1_free[b3] = tkl
            if kind == "w1":
                w1col[job[1]] = tuple(cps)
            elif job[1] == KD - 1:
                w2grp[job[2]] = tuple(cps)

    def wq_flush():
        while pend_T:
            wq_back(pend_T.pop(0))

    pend_T = []
    pend_lim = [1]
    for n in range(KD + 1):
        wq_job(n)

    xin_free = [None, None]
    xn_free = [None] * NRING
    one_free = {"xnb": None, "xqb": None, "qt": None, "qi": None, "attnb": None}
    xnfm_free = [None, None]
    xnb_wb = [None] * NTILE
    xq_wb = [None] * NTILE
    attn_wb = [None] * NTILE
    xn_ready = [None] * NTILE
    pbank_free = [None] * 8
    ln_prev = {}
    sp_prev = {}

    def phaseA_stage1(t):
        bi = t % 2
        ld = pl.op("sync", lambda e, t=t, bi=bi: e.dma_start(
            out=xin[bi][:, :], in_=x[t * P:(t + 1) * P, :]),
            waits=(xin_free[bi],), sem="dx%d" % bi, inc=16)
        s1 = pl.op("vector", lambda e, bi=bi: e.bn_stats(bn12[bi][:, 0:6], xin[bi][:, 0:512]),
                   waits=(ld, ln_prev.get("aggr%d" % bi)), sem="cv")
        s2 = pl.op("vector", lambda e, bi=bi: e.bn_stats(bn12[bi][:, 6:12], xin[bi][:, 512:1024]),
                   waits=(s1,), sem="cv")
        ag = pl.op("vector", lambda e, bi=bi: e.bn_aggr(bn2[bi][:, :], bn12[bi][:, :]),
                   waits=(s2, ln_prev.get("apply%d" % bi)), sem="cv")
        ln_prev["aggr%d" % bi] = ag
        l1 = pl.op("scalar", lambda e, bi=bi: e.activation(lnv[bi][:, :], bn2[bi][:, 1:2], AF.Ln, bias=ceps[:, :]),
                   waits=(ag, consts_tok, ln_prev.get("apply%d" % bi)), sem="cs")
        l2 = pl.op("scalar", lambda e, bi=bi: e.activation(rstd[bi][:, :], lnv[bi][:, :], AF.Exp, scale=cmhalf[:, :]),
                   waits=(l1,), sem="cs")
        return l2

    def phaseA_stage2(t, l2):
        bi = t % 2
        ri = t % NRING
        ap_ = pl.op("vector", lambda e, bi=bi, ri=ri: e.tensor_scalar(
            xn_tm[ri][:, :], xin[bi][:, :], bn2[bi][:, 0:1], rstd[bi][:, :], ALU.subtract, ALU.mult),
            waits=(l2, xn_free[ri]), sem="cv")
        ln_prev["apply%d" % bi] = ap_
        xin_free[bi] = ap_
        xn_ready[t] = ap_
        cb = pl.op("scalar", lambda e, ri=ri: e.activation(xnb[:, :], xn_tm[ri][:, :], AF.Copy),
                   waits=(ap_, one_free["xnb"]), sem="cs")
        wb1 = pl.op("gpsimd", lambda e, t=t: e.dma_start(
            out=s_xnb[t * P:(t + 1) * P, :], in_=xnb[:, :]),
            waits=(cb,), sem="dwbx", inc=16)
        one_free["xnb"] = wb1
        xnb_wb[t] = wb1
        q3 = None
        for h in range(2):
            q1 = pl.op("vector", lambda e, ri=ri, h=h: e.tensor_scalar(
                qt[:, :], xn_tm[ri][:, h * 512:(h + 1) * 512], 16.0, CMAG, ALU.mult, ALU.add),
                waits=(ap_, one_free["qt"]), sem="cv")
            q2 = pl.op("vector", lambda e: e.tensor_scalar(
                qi[:, :], qt[:, :], -CMAG, 127.0, ALU.add, ALU.min),
                waits=(q1, one_free["qi"]), sem="cv")
            one_free["qt"] = q2
            q3 = pl.op("vector", lambda e, h=h: e.tensor_scalar(
                xqb[:, h * 512:(h + 1) * 512], qi[:, :], -128.0, 1.0 / 16.0, ALU.max, ALU.mult),
                waits=(q2, one_free["xqb"] if h == 0 else None), sem="cv")
            one_free["qi"] = q3
        wb2 = pl.op("gpsimd", lambda e, t=t: e.dma_start(
            out=s_xq[t * P:(t + 1) * P, :], in_=xqb[:, :]),
            waits=(q3,), sem="dwbq", inc=16)
        one_free["xqb"] = wb2
        xq_wb[t] = wb2

    def phaseA_pair_tp(p):
        pbuf = p % 2
        tok = None
        for k in range(KD):
            tok = pl.op("sync", lambda e, k=k, p=p, pbuf=pbuf: e.dma_start(
                out=xn_fm[pbuf][:, k * 256:(k + 1) * 256],
                in_=s_xnb[p * 256:(p + 1) * 256, k * P:(k + 1) * P], transpose=True),
                waits=(xnb_wb[2 * p + 1], xnfm_free[pbuf]), sem="txn%d" % pbuf, inc=16)
        return tok

    def phaseA_delta(t, fm_tok):
        pbuf = (t // 2) % 2
        jj = t % 2
        par = t % 2
        bk0, bk1, bkc = (0, 1, 4) if par == 0 else (2, 3, 5)
        mw = [fm_tok, dsm_all, consts_tok,
              pbank_free[bk0], pbank_free[bk1], pbank_free[bkc]] + \
            list(W1T_ready[0] or ()) + [W23T_ready[0]]
        pl.op("tensor", lambda e, bkc=bkc: e.matmul(
            pb[bkc][:, 0:32], ones64[:, :], b23row[:, :], start=True, stop=False), waits=tuple(mw))
        pl.op("tensor", lambda e, bk0=bk0: e.matmul(
            pb[bk0][:, :], onesr[:, :], b1row[:, 0:512], start=True, stop=False))
        pl.op("tensor", lambda e, bk1=bk1: e.matmul(
            pb[bk1][:, :], onesr[:, :], b1row[:, 512:1024], start=True, stop=False))
        mm_tok = None
        for k in range(KD):
            last = k == KD - 1
            lo = k * 256 + jj * P
            pl.op("tensor", lambda e, k=k, lo=lo, bk0=bk0, pbuf=pbuf, last=last: e.matmul(
                pb[bk0][:, :], xn_fm[pbuf][:, lo:lo + P], W1T[:, k * D:k * D + 512],
                start=False, stop=last))
            pl.op("tensor", lambda e, k=k, lo=lo, bk1=bk1, pbuf=pbuf, last=last: e.matmul(
                pb[bk1][:, :], xn_fm[pbuf][:, lo:lo + P], W1T[:, k * D + 512:(k + 1) * D],
                start=False, stop=last))
            mm_tok = pl.op("tensor", lambda e, k=k, lo=lo, bkc=bkc, pbuf=pbuf, last=last: e.matmul(
                pb[bkc][:, 0:32], xn_fm[pbuf][:, lo:lo + P], W23T[:, k * 32:(k + 1) * 32],
                start=False, stop=last), sem="pe" if last else None)
        ebc = pl.op("scalar", lambda e, bkc=bkc: e.activation(evbc[:, :], pb[bkc][:, 0:32], AF.Copy),
                    waits=(mm_tok, sp_prev.get("evbc_free")), sem="cs")
        pr = pl.op("vector", lambda e: e.tensor_tensor(prodbc[:, :], evbc[:, 0:16], evbc[:, 16:32], ALU.mult),
                   waits=(ebc, sp_prev.get("prod_free")), sem="cv")
        sp_prev["evbc_free"] = pr
        rb = pl.op("vector", lambda e: e.reduce_sum(bcv[:, :], prodbc[:, :], AX.X),
                   waits=(pr, sp_prev.get("bc_free")), sem="cv")
        sp_prev["prod_free"] = rb
        at_tok = None
        slice_last = {}
        for s in range(2):
            bks = bk0 if s == 0 else bk1
            par2 = t % 2
            u = pl.op("scalar", lambda e, bks=bks, par=par2: e.activation(
                sp_u[par][:, :], pb[bks][:, :], AF.Abs),
                waits=(mm_tok, sp_prev.get("u_free%d" % par2)), sem="cs")
            v = pl.op("scalar", lambda e, par=par2: e.activation(
                sp_v[par][:, :], sp_u[par][:, :], AF.Exp, scale=-1.0),
                waits=(u, sp_prev.get("v_free%d" % par2)), sem="cs")
            w = pl.op("scalar", lambda e, par=par2: e.activation(
                sp_u[par][:, :], sp_v[par][:, :], AF.Ln, bias=1.0),
                waits=(v,), sem="cs")
            d = pl.op("vector", lambda e, bks=bks, par=par2: e.scalar_tensor_tensor(
                sp_v[par][:, :], pb[bks][:, :], 0.0, sp_u[par][:, :], ALU.max, ALU.add),
                waits=(w,), sem="cv")
            sp_prev["u_free%d" % par2] = d
            at_tok = pl.op("vector", lambda e, s=s, par=par2, ri=t % NRING: e.scalar_tensor_tensor(
                attnb[:, s * 512:(s + 1) * 512], sp_v[par][:, :], bcv[:, :],
                xn_tm[ri][:, s * 512:(s + 1) * 512], ALU.mult, ALU.mult),
                waits=(d, rb, xn_ready[t], one_free["attnb"]), sem="cv")
            sp_prev["v_free%d" % par2] = at_tok
            sp_prev["bc_free"] = at_tok
            slice_last[bks] = at_tok
        xn_free[t % NRING] = at_tok
        wb = pl.op("gpsimd", lambda e, t=t: e.dma_start(
            out=s_attn[t * P:(t + 1) * P, :], in_=attnb[:, :]),
            waits=(at_tok,), sem="dwba", inc=16)
        one_free["attnb"] = wb
        attn_wb[t] = wb
        pbank_free[bk0] = slice_last[bk0]
        pbank_free[bk1] = slice_last[bk1]
        pbank_free[bkc] = ebc
        return mm_tok

    njobs = len(wq_jobs)
    job_n = KD + 1
    per_pair = (njobs - job_n + NPAIR - 1) // NPAIR
    pending = None
    for p in range(NPAIR):
        l2a = phaseA_stage1(2 * p)
        l2b = phaseA_stage1(2 * p + 1)
        phaseA_stage2(2 * p, l2a)
        phaseA_stage2(2 * p + 1, l2b)
        fm_tok = phaseA_pair_tp(p)
        if pending is not None:
            pp, pfm = pending
            phaseA_delta(2 * pp, pfm)
            xnfm_free[pp % 2] = phaseA_delta(2 * pp + 1, pfm)
        pending = (p, fm_tok)
        for _ in range(per_pair):
            if job_n < njobs:
                wq_job(job_n)
                job_n += 1
    while job_n < njobs:
        wq_job(job_n)
        job_n += 1
    pp, pfm = pending
    phaseA_delta(2 * pp, pfm)
    last_delta_mm = phaseA_delta(2 * pp + 1, pfm)
    xnfm_free[pp % 2] = last_delta_mm
    last_delta[0] = last_delta_mm
    for n in range(njobs, njobs + 3):
        wq_job(n)
    defer_n = [njobs + 3]

    xqfm_free = [None, None]
    hq_free = [one_free["xnb"], one_free["xnb"]]
    epit_free = [one_free["xnb"]]
    epii_free = one_free["xqb"]
    xre_free = xn_free[0]
    atre_free = xn_free[1]
    t1_free = [xn_free[2], xn_free[2]]
    hbank_free = pbank_free

    def emit_xq_tp(g):
        gb = g % 2
        tok = None
        for k in range(KD):
            tok = pl.op("sync", lambda e, k=k, g=g, gb=gb: e.dma_start(
                out=xq_fm[gb][:, k * 256:(k + 1) * 256],
                in_=s_xq[g * 256:(g + 1) * 256, k * P:(k + 1) * P], transpose=True),
                waits=(xq_wb[2 * g + 1], xqfm_free[gb],
                       xin_free[gb] if g < 2 else None), sem="txq%d" % gb, inc=16)
        return tok

    xq_toks = [None] * NGRP
    xq_toks[0] = emit_xq_tp(0)
    for g in range(NGRP):
        gb = g % 2
        PO = [0, 1, 2, 3] if g % 2 == 0 else [4, 5, 6, 7]
        HB = [4, 5] if g % 2 == 0 else [0, 1]
        if g + 1 < NGRP:
            xq_toks[g + 1] = emit_xq_tp(g + 1)
        xq_tok = xq_toks[g]
        pw = [xq_tok, b2q_tok, b1q_tok, consts_tok] + \
             [pbank_free[b] for b in PO] + [pbank_free[b] for b in HB]
        for tt in range(2):
            for s in range(2):
                pl.op("tensor", lambda e, tt=tt, s=s, PO=PO: e.matmul(
                    pb[PO[2 * tt + s]][:, :], ones32[:, :], b2qrow[:, s * 512:(s + 1) * 512],
                    start=True, stop=False),
                    waits=tuple(pw) if (tt == 0 and s == 0) else ())
        lx0 = pl.op("sync", lambda e, t=2 * g: e.dma_start(
            out=xre[:, :], in_=x[t * P:(t + 1) * P, :]),
            waits=(xre_free,), sem="drx", inc=16)
        la0 = pl.op("sync", lambda e, t=2 * g: e.dma_start(
            out=atre[:, :], in_=s_attn[t * P:(t + 1) * P, :]),
            waits=(atre_free, attn_wb[2 * g]), sem="dra", inc=16)
        ffn2_bank_tok = [None] * 4
        last_xqfm_mm = None

        def emit_f2(j, hb, ep3, last_j):
            ftok = None
            for tt in range(2):
                for s in range(2):
                    need_sem = (tt == 1 and s == 1) or last_j
                    ftok = pl.op("tensor", lambda e, tt=tt, s=s, j=j, hb=hb, last_j=last_j, PO=PO: e.matmul(
                        pb[PO[2 * tt + s]][:, :], hq[hb][:, tt * P:(tt + 1) * P],
                        w2qT[:, j * D + s * 512:j * D + (s + 1) * 512],
                        start=False, stop=last_j),
                        waits=((ep3,) if (tt == 0 and s == 0) else ()) +
                              (w2grp[j // 8] or ()),
                        sem="pe" if need_sem else None)
                    if last_j:
                        ffn2_bank_tok[2 * tt + s] = ftok
            hq_free[hb] = ftok

        prev_f2 = None
        for j in range(KH):
            hb = j % 2
            bkh = HB[hb]
            h_done = None
            KORDER = (4, 5, 6, 7, 2, 3, 0, 1)
            for ki, k in enumerate(KORDER):
                last = ki == KD - 1
                h_done = pl.op("tensor", lambda e, k=k, j=j, bkh=bkh, gb=gb, last=last, first=ki == 0: e.matmul(
                    pb[bkh][:, 0:256], w1qT[:, k * HID + j * P:k * HID + (j + 1) * P],
                    xq_fm[gb][:, k * 256:(k + 1) * 256],
                    start=first, stop=last),
                    waits=((hbank_free[bkh], w1col[j][1]) if ki == 0 else
                           ((w1col[j][0],) if ki == 4 else ())),
                    sem="pe" if last else None)
            ep1 = pl.op("scalar", lambda e, j=j, bkh=bkh, hb=hb: e.activation(
                epi_t[hb][:, :], pb[bkh][:, 0:256], AF.Relu, bias=b1q16[:, j:j + 1], scale=c16[:, :]),
                waits=(h_done, b1q_tok, epit_free[0]), sem="cs")
            hbank_free[bkh] = ep1
            ep2 = pl.op("vector", lambda e, hb=hb: e.tensor_scalar(
                epi_i[:, :], epi_t[hb][:, :], CMAG, -CMAG, ALU.add, ALU.add),
                waits=(ep1, epii_free), sem="cv")
            epit_free[0] = ep2
            ep3 = pl.op("vector", lambda e, hb=hb: e.tensor_scalar(
                hq[hb][:, :], epi_i[:, :], 127.0, 1.0 / 16.0, ALU.min, ALU.mult),
                waits=(ep2, hq_free[hb]), sem="cv")
            epii_free = ep3
            pbank_free[bkh] = ep1
            if prev_f2 is not None:
                emit_f2(*prev_f2)
            prev_f2 = (j, hb, ep3, j == KH - 1)
            last_xqfm_mm = h_done
            if g == 0 and defer_n[0] < len(all_jobs) and j >= 1:
                wq_job(defer_n[0])
                defer_n[0] += 1
                if defer_n[0] == len(all_jobs):
                    wq_flush()
        emit_f2(*prev_f2)
        xqfm_free[gb] = last_xqfm_mm
        if g == 0:
            wq_flush()
            pbank_free[6] = pbank67_free[0]
            pbank_free[7] = pbank67_free[1]
        for tt in range(2):
            t = 2 * g + tt
            if tt == 0:
                lx, la = lx0, la0
            else:
                lx = pl.op("sync", lambda e, t=t: e.dma_start(
                    out=xre[:, :], in_=x[t * P:(t + 1) * P, :]),
                    waits=(xre_free,), sem="drx", inc=16)
                la = pl.op("sync", lambda e, t=t: e.dma_start(
                    out=atre[:, :], in_=s_attn[t * P:(t + 1) * P, :]),
                    waits=(atre_free, attn_wb[t]), sem="dra", inc=16)
            yhs = []
            for s in range(2):
                bk = 2 * tt + s
                t1 = pl.op("vector", lambda e, bk=bk, s=s, PO=PO: e.tensor_tensor(
                    t1b[s][:, :], pb[PO[bk]][:, :], atre[:, s * 512:(s + 1) * 512], ALU.add),
                    waits=(ffn2_bank_tok[bk], la, t1_free[s]), sem="cv")
                pbank_free[PO[bk]] = t1
                yh = pl.op("vector", lambda e, s=s: e.tensor_tensor(
                    t1b[s][:, :], t1b[s][:, :],
                    xre[:, s * 512:(s + 1) * 512], ALU.add),
                    waits=(t1, lx), sem="cv")
                yhs.append(yh)
                st_ = pl.op("gpsimd", lambda e, t=t, s=s: e.dma_start(
                    out=y[t * P:(t + 1) * P, s * 512:(s + 1) * 512], in_=t1b[s][:, :]),
                    waits=(yh,), sem="dst%d" % s, inc=16)
                t1_free[s] = st_
            xre_free = yhs[-1]
            atre_free = yhs[-1]

    pl.raw("gpsimd", lambda e, S: e.wait_ge(S["dst0"], 16 * NTILE))
    pl.raw("gpsimd", lambda e, S: e.wait_ge(S["dst1"], 16 * NTILE))

    with contextlib.ExitStack() as ctx2:
        S = {}
        for s in SEMS:
            S[s] = ctx2.enter_context(nc.semaphore(s))
        block = ctx2.enter_context(nc.Block())
        block.sync(lambda e: [f(e, S) for f in pl.ops["sync"]])
        block.scalar(lambda e: [f(e, S) for f in pl.ops["scalar"]])
        block.vector(lambda e: [f(e, S) for f in pl.ops["vector"]])
        block.tensor(lambda e: [f(e, S) for f in pl.ops["tensor"]])
        block.gpsimd(lambda e: [f(e, S) for f in pl.ops["gpsimd"]])
    stack.close()
    return nc


_NC_CACHE = [None]


def kernel(L0_attn_x, L0_attn_W1, L0_attn_b1, L0_attn_W2, L0_attn_b2,
           L0_attn_W3, L0_attn_b3, L0_attn_A,
           L0_ffn_w1, L0_ffn_b1, L0_ffn_w2, L0_ffn_b2):
    xf = np.ascontiguousarray(np.asarray(L0_attn_x, dtype=np.float32).reshape(NCORES * T, D))
    com = dict(
        W1=np.ascontiguousarray(np.asarray(L0_attn_W1, dtype=np.float32)),
        b1=np.ascontiguousarray(np.asarray(L0_attn_b1, dtype=np.float32).reshape(1, D)),
        W2=np.ascontiguousarray(np.asarray(L0_attn_W2, dtype=np.float32)),
        b2=np.ascontiguousarray(np.asarray(L0_attn_b2, dtype=np.float32).reshape(1, NST)),
        W3=np.ascontiguousarray(np.asarray(L0_attn_W3, dtype=np.float32)),
        b3=np.ascontiguousarray(np.asarray(L0_attn_b3, dtype=np.float32).reshape(1, NST)),
        fw1=np.ascontiguousarray(np.asarray(L0_ffn_w1, dtype=np.float32)),
        fb1=np.ascontiguousarray(np.asarray(L0_ffn_b1, dtype=np.float32).reshape(HID, 1)),
        fw2=np.ascontiguousarray(np.asarray(L0_ffn_w2, dtype=np.float32)),
        fb2=np.ascontiguousarray(np.asarray(L0_ffn_b2, dtype=np.float32).reshape(1, D)),
    )
    if _NC_CACHE[0] is None:
        _NC_CACHE[0] = build_nc()
    nc = _NC_CACHE[0]
    in_maps = [dict(com, x=xf[c * T:(c + 1) * T]) for c in range(NCORES)]
    res = run_bass_kernel_spmd(nc, in_maps, core_ids=list(range(NCORES)))
    out = np.concatenate([np.asarray(res.results[c]["y"]) for c in range(NCORES)], axis=0)
    return out.reshape(4, 4096, D).astype(np.float32, copy=False)


if __name__ == "__main__":
    nc = build_nc()
    print("build ok")


# version 37
# speedup vs baseline: 1.0026x; 1.0026x over previous
import numpy as np
import concourse.bass as bass
import concourse.mybir as mybir
from concourse.bass_utils import run_bass_kernel_spmd

F32 = mybir.dt.float32
BF16 = mybir.dt.bfloat16
I32 = mybir.dt.int32
AF = mybir.ActivationFunctionType
ALU = mybir.AluOpType
AX = mybir.AxisListType

P = 128
D = 1024
HID = 4096
NST = 16
NCORES = 8
T = 2048
NTILE = T // P
NPAIR = NTILE // 2
NGRP = NTILE // 2
KD = D // P
KH = HID // P
CMAG = 12582912.0

ENGINES = ("sync", "scalar", "vector", "tensor", "gpsimd")


class Plan:
    def __init__(self):
        self.ops = {e: [] for e in ENGINES}
        self.cnt = {}
        self.waited = {}

    def sem(self, name):
        self.cnt[name] = 0

    def op(self, eng, build, waits=(), sem=None, inc=1):
        for w in waits:
            if w is None:
                continue
            s, v = w
            key = (eng, s)
            if self.waited.get(key, -1) < v:
                self.ops[eng].append(lambda e, S, s=s, v=v: e.wait_ge(S[s], v))
                self.waited[key] = v
        if sem is not None:
            self.cnt[sem] += inc
            v2 = self.cnt[sem]
            self.ops[eng].append(lambda e, S, b=build, s=sem, n=inc: b(e).then_inc(S[s], n))
            return (sem, v2)
        self.ops[eng].append(lambda e, S, b=build: b(e))
        return None

    def raw(self, eng, fn):
        self.ops[eng].append(fn)


def build_nc():
    nc = bass.Bass("TRN2", target_bir_lowering=False, debug=False)

    x = nc.dram_tensor("x", [T, D], F32, kind="ExternalInput").ap()
    W1 = nc.dram_tensor("W1", [D, D], F32, kind="ExternalInput").ap()
    b1 = nc.dram_tensor("b1", [1, D], F32, kind="ExternalInput").ap()
    W2 = nc.dram_tensor("W2", [NST, D], F32, kind="ExternalInput").ap()
    b2 = nc.dram_tensor("b2", [1, NST], F32, kind="ExternalInput").ap()
    W3 = nc.dram_tensor("W3", [NST, D], F32, kind="ExternalInput").ap()
    b3 = nc.dram_tensor("b3", [1, NST], F32, kind="ExternalInput").ap()
    fw1 = nc.dram_tensor("fw1", [HID, D], F32, kind="ExternalInput").ap()
    fb1 = nc.dram_tensor("fb1", [HID, 1], F32, kind="ExternalInput").ap()
    fw2 = nc.dram_tensor("fw2", [D, HID], F32, kind="ExternalInput").ap()
    fb2 = nc.dram_tensor("fb2", [1, D], F32, kind="ExternalInput").ap()

    y = nc.dram_tensor("y", [T, D], F32, kind="ExternalOutput").ap()

    s_xnb = nc.dram_tensor("s_xnb", [T, D], BF16, kind="Internal").ap()
    s_xq = nc.dram_tensor("s_xq", [T, D], BF16, kind="Internal").ap()
    s_attn = nc.dram_tensor("s_attn", [T, D], BF16, kind="Internal").ap()

    SBUF_TOP = 229376 - 32
    BASE = 16384 + 1024
    cur = [BASE]
    off = {}

    def at(name, shape, dtype, offset=None):
        o = cur[0] if offset is None else offset
        h = nc.alloc_sbuf_tensor_at(name, shape, dtype, offset=o)
        if offset is None:
            eb = {F32: 4, BF16: 2, I32: 4}[dtype]
            n = 1
            for s in shape[1:]:
                n *= s
            off[name] = o
            cur[0] = o + ((n * eb + 31) // 32) * 32
        return h.ap()

    W23T = at("W23T", [P, KD * 32], BF16)
    ident = at("ident", [P, P], F32)
    rows_a = at("rows_a", [P, D], F32)
    b1row = rows_a[0:1, :]
    b2qrow = rows_a[32:33, :]
    fb2row = rows_a[64:65, :]
    rows_b = at("rows_b", [P, 160], F32)
    onesr = rows_b[0:1, 0:128]
    ones32 = rows_b[32:33, 0:128]
    ones64 = rows_b[64:65, 0:128]
    b23row = rows_b[64:65, 128:160]
    b1q16 = at("b1q16", [P, KH], F32)
    b1qi = at("b1qi", [P, KH], I32)
    fb1c = at("fb1c", [P, KH], F32)
    cC = at("cC", [P, 1], F32)
    c64 = at("c64", [P, 1], F32)
    c16 = at("c16", [P, 1], F32)
    ceps = at("ceps", [P, 1], F32)
    cmhalf = at("cmhalf", [P, 1], F32)
    c164 = at("c164", [P, 1], F32)
    bn12 = [at("bn12_%d" % i, [P, 12], F32) for i in range(2)]
    bn2 = [at("bn2_%d" % i, [P, 2], F32) for i in range(2)]
    lnv = [at("lnv_%d" % i, [P, 1], F32) for i in range(2)]
    rstd = [at("rstd_%d" % i, [P, 1], F32) for i in range(2)]
    fbst = at("fbst", [32, P], F32)
    bcv = at("bcv", [P, 1], F32)
    evbc = at("evbc", [P, 32], F32)
    prodbc = at("prodbc", [P, 16], F32)

    xin = [at("xin%d" % i, [P, D], F32) for i in range(2)]
    xn_tm = [at("xn%d" % i, [P, D], F32) for i in range(4)]
    NRING = 4
    xnb = at("xnb", [P, D], BF16)
    xqb = at("xqb", [P, D], BF16)
    qt = at("qt", [P, 512], F32)
    qi = at("qi", [P, 512], I32)
    xn_fm = [at("xnfm%d" % i, [P, KD * 256], BF16) for i in range(2)]
    sp_u = [at("sp_u%d" % i, [P, 512], F32) for i in range(2)]
    sp_v = [at("sp_v%d" % i, [P, 512], F32) for i in range(2)]
    attnb = at("attnb", [P, D], BF16)
    wld = [at("wld%d" % i, [P, D], F32) for i in range(2)]
    wt1 = [at("wt1_%d" % i, [P, D], F32) for i in range(3)]
    wi1 = [at("wi1_%d" % i, [P, D], I32, offset=off["wt1_%d" % i]) for i in range(3)]

    xq_fm = [at("xqfm%d" % i, [P, KD * 256], BF16, offset=off["xin%d" % i]) for i in range(2)]
    hq = [at("hq%d" % i, [P, 256], BF16, offset=off["xnb"] + i * 512) for i in range(2)]
    epi_t0 = at("epit", [P, 256], F32, offset=off["xnb"] + 1024)
    epi_t = [epi_t0, epi_t0]
    epi_i = at("epii", [P, 256], I32, offset=off["xqb"])
    xre = at("xre", [P, D], F32, offset=off["xn0"])
    atre = at("atre", [P, D], BF16, offset=off["xn1"])
    t1b = [at("t1b%d" % i, [P, 512], F32, offset=off["xn2"] + i * 2048) for i in range(2)]

    WR = SBUF_TOP - 131072
    assert cur[0] <= WR, f"main region overflows into weight region: {cur[0]} > {WR}"
    w1qT = at("w1qT", [P, KD * HID], BF16, offset=WR)
    w1qT3 = w1qT.rearrange("p (k h) -> p k h", k=KD)
    w2qT = at("w2qT", [P, KH * D], BF16, offset=WR + 65536)
    w2qT3 = w2qT.rearrange("p (k h) -> p k h", k=KH)
    W1T = at("W1T", [P, KD * D], BF16, offset=WR + 65536 + 49152)
    W1T3 = W1T.rearrange("p (k h) -> p k h", k=KD)

    import contextlib
    stack = contextlib.ExitStack()
    pb = [stack.enter_context(nc.psum_tensor("pb%d" % i, [P, 512], F32)) for i in range(8)]

    pl = Plan()
    SEMS = ("dsm", "dwl0", "dwl1", "dx0", "dx1", "txn0", "txn1", "txn2", "txn3", "txq0", "txq1",
            "dwbx", "dwbq", "dwba", "drx", "dra", "dst0", "dst1", "cv", "cs", "pe", "cg")
    for s in SEMS:
        pl.sem(s)

    for ap_, val in ((cC, CMAG), (c64, 64.0), (c16, 16.0), (ceps, 1e-5),
                     (cmhalf, -0.5), (c164, 1.0 / 64.0)):
        nc.const_aps.register(F32, val, ap_[:, :])
        pl.op("vector", lambda e, a=ap_, v=val: e.memset(a[:, :], v))
    pl.op("vector", lambda e: e.memset(rows_b[:, 0:128], 1.0))
    consts_tok = pl.op("vector", lambda e: e.memset(b23row[:, :], 0.0), sem="cv")
    cg_tok = pl.op("gpsimd", lambda e: e.memset(ident[:, :], 0.0), sem="cg")
    cg_tok = pl.op("gpsimd", lambda e: e.affine_select(
        out=ident[:, :], in_=ident[:, :], compare_op=ALU.not_equal, fill=1.0,
        base=0, pattern=[[-1, P]], channel_multiplier=1),
        waits=(cg_tok,), sem="cg")

    pl.op("sync", lambda e: e.dma_start(out=b1row[:, :], in_=b1[:, :]), sem="dsm", inc=16)
    pl.op("sync", lambda e: e.dma_start(out=b23row[:, 0:NST], in_=b2[:, :]),
          waits=(consts_tok,), sem="dsm", inc=16)
    pl.op("sync", lambda e: e.dma_start(out=b23row[:, NST:2 * NST], in_=b3[:, :]),
          sem="dsm", inc=16)
    pl.op("sync", lambda e: e.dma_start(out=fb2row[:, :], in_=fb2[:, :]), sem="dsm", inc=16)
    fb1_ld = pl.op("sync", lambda e: e.dma_start(
        out=fbst[:, :], in_=fb1.rearrange("(a b) one -> a (b one)", a=KH)),
        sem="dsm", inc=16)
    dsm_all = ("dsm", 16 * 5)
    fb1_tp = pl.op("tensor", lambda e: e.transpose(
        pb[6][:, 0:KH], fbst[:, :], ident[0:32, 0:32]),
        waits=(fb1_ld, cg_tok), sem="pe")
    fb1_tok = pl.op("scalar", lambda e: e.activation(
        fb1c[:, :], pb[6][:, 0:KH], AF.Copy), waits=(fb1_tp,), sem="cs")

    t = pl.op("vector", lambda e: e.tensor_scalar(b2qrow[:, :], fb2row[:, :], 64.0, CMAG, ALU.mult, ALU.add),
              waits=(dsm_all, consts_tok), sem="cv")
    t = pl.op("vector", lambda e: e.tensor_scalar(wi1[0][0:1, 0:D], b2qrow[:, :], -CMAG, 127.0, ALU.add, ALU.min),
              waits=(t,), sem="cv")
    b2q_tok = pl.op("vector", lambda e: e.tensor_scalar(b2qrow[:, :], wi1[0][0:1, 0:D], -128.0, 1.0 / 64.0, ALU.max, ALU.mult),
                    waits=(t,), sem="cv")
    t = pl.op("vector", lambda e: e.tensor_scalar(b1q16[:, :], fb1c[:, :], 64.0, CMAG, ALU.mult, ALU.add),
              waits=(fb1_tok,), sem="cv")
    t = pl.op("vector", lambda e: e.tensor_scalar(b1qi[:, :], b1q16[:, :], -CMAG, 127.0, ALU.add, ALU.min),
              waits=(t,), sem="cv")
    t = pl.op("vector", lambda e: e.tensor_scalar(b1q16[:, :], b1qi[:, :], -128.0, 1.0 / 64.0, ALU.max, ALU.mult),
              waits=(t,), sem="cv")
    b1q_tok = pl.op("vector", lambda e: e.tensor_scalar_mul(b1q16[:, :], b1q16[:, :], 16.0),
                    waits=(t,), sem="cv")

    wq_jobs = [("w1c", i) for i in range(KD)] + [("w23",)] + \
              [("w1", i) for i in range(KH)] + \
              [("w2", i, h) for h in range(3) for i in range(KD)]
    wq_defer = [("w2", i, 3) for i in range(KD)]
    all_jobs = wq_jobs + wq_defer

    wld_free = [None, None]
    wt1_free = [b2q_tok, None, None]
    pbank67_free = [fb1_tok, None]
    lt_toks = [None] * len(all_jobs)
    w1col = [None] * KH
    w2grp = [None] * 4
    W1T_ready = [None]
    W23T_ready = [None]
    last_delta = [None]

    def wq_load(n):
        job = all_jobs[n]
        bi = n % 2
        if job[0] == "w1c":
            i = job[1]
            lt_toks[n] = pl.op("scalar", lambda e, i=i, bi=bi: e.dma_start(
                out=wld[bi][:, :], in_=W1[i * P:(i + 1) * P, :]),
                waits=(wld_free[bi],), sem="dwl%d" % bi, inc=16)
        elif job[0] == "w23":
            pl.op("scalar", lambda e, bi=bi: e.dma_start(
                out=wld[bi][0:NST, :], in_=W2[:, :]),
                waits=(wld_free[bi],), sem="dwl%d" % bi, inc=16)
            lt_toks[n] = pl.op("scalar", lambda e, bi=bi: e.dma_start(
                out=wld[bi][NST:2 * NST, :], in_=W3[:, :]), sem="dwl%d" % bi, inc=16)
        elif job[0] == "w1":
            i = job[1]
            lt_toks[n] = pl.op("scalar", lambda e, i=i, bi=bi: e.dma_start(
                out=wld[bi][:, :], in_=fw1[i * P:(i + 1) * P, :]),
                waits=(wld_free[bi],), sem="dwl%d" % bi, inc=16)
        else:
            i, h = job[1], job[2]
            lt_toks[n] = pl.op("scalar", lambda e, i=i, h=h, bi=bi: e.dma_start(
                out=wld[bi][:, :], in_=fw2[i * P:(i + 1) * P, h * D:(h + 1) * D]),
                waits=(wld_free[bi],), sem="dwl%d" % bi, inc=16)

    def wq_job(n):
        job = all_jobs[n]
        bi = n % 2
        if lt_toks[n] is None:
            wq_load(n)
        lt = lt_toks[n]
        kind = job[0]
        if kind == "w23":
            pend_T.append((job, kind, wld[bi], lt, 0, None, 0, bi, 0))
            if len(pend_T) > 1:
                wq_back(pend_T.pop(0))
            if n + 1 < len(all_jobs) and lt_toks[n + 1] is None:
                wq_load(n + 1)
            return
        b3 = n % 3
        if kind == "w1c":
            src = wld[bi]
            ready = lt
            i = job[1]
            dst3, c_base = W1T3, 0
        else:
            p1 = pl.op("scalar", lambda e, bi=bi, b3=b3: e.activation(
                wt1[b3][:, :], wld[bi][:, :], AF.Copy, bias=CMAG, scale=64.0),
                waits=(lt, consts_tok, wt1_free[b3]), sem="cs")
            wld_free[bi] = p1
            p2 = pl.op("vector", lambda e, b3=b3: e.tensor_scalar(
                wi1[b3][:, :], wt1[b3][:, :], -CMAG, 127.0, ALU.add, ALU.min),
                waits=(p1,), sem="cv")
            p3 = pl.op("vector", lambda e, b3=b3: e.tensor_scalar(
                wt1[b3][:, :], wi1[b3][:, :], -128.0, 1.0, ALU.max, ALU.mult),
                waits=(p2,), sem="cv")
            src = wt1[b3]
            ready = p3
            i = job[1]
            if kind == "w1":
                dst3, c_base = w1qT3, 0
            else:
                dst3, c_base = w2qT3, 8 * job[2]
        pend_T.append((job, kind, src, ready, i, dst3, c_base, bi, b3))
        if len(pend_T) > 1:
            wq_back(pend_T.pop(0))
        if n + 1 < len(all_jobs) and lt_toks[n + 1] is None:
            wq_load(n + 1)

    def wq_back(ctx):
        job, kind, src, ready, i, dst3, c_base, bi, b3 = ctx
        if kind == "w23":
            tk = None
            for c in range(KD):
                tk = pl.op("tensor", lambda e, c=c, src=src: e.transpose(
                    pb[6][:, c * 32:(c + 1) * 32], src[0:32, c * P:(c + 1) * P],
                    ident[0:32, 0:32]),
                    waits=(ready, cg_tok, pbank67_free[0]) if c == 0 else (),
                    sem="pe" if c == KD - 1 else None)
            cp = pl.op("scalar", lambda e: e.activation(
                W23T[:, :], pb[6][:, 0:KD * 32], AF.Copy),
                waits=(tk,), sem="cs")
            pbank67_free[0] = cp
            wld_free[bi] = tk
            W23T_ready[0] = cp
            return
        h3 = kind == "w2" and job[2] == 3
        cps = []
        tkl = None
        for w in range(2):
            for c4 in range(4):
                c = 4 * w + c4
                tkl = pl.op("tensor", lambda e, c=c, c4=c4, w=w, src=src: e.transpose(
                    pb[6 + w][:, c4 * 128:(c4 + 1) * 128], src[:, c * P:(c + 1) * P],
                    ident[:, :]),
                    waits=((ready, cg_tok, pbank67_free[w]) if c4 == 0 else ()),
                    sem="pe" if c4 == 3 else None)
            dd = dst3[:, c_base + 4 * w:c_base + 4 * w + 4, i * P:(i + 1) * P]
            xw = (tkl,) + ((last_delta[0],) if h3 else ())
            if w == 0:
                if kind == "w1c":
                    cp = pl.op("scalar", lambda e, dd=dd: e.activation(
                        dd, pb[6][:, :], AF.Copy),
                        waits=xw, sem="cs")
                else:
                    cp = pl.op("scalar", lambda e, dd=dd: e.activation(
                        dd, pb[6][:, :], AF.Copy, scale=c164[:, :]),
                        waits=xw + (consts_tok,), sem="cs")
            else:
                if kind == "w1c":
                    cp = pl.op("vector", lambda e, dd=dd: e.tensor_copy(dd, pb[7][:, :]),
                               waits=xw, sem="cv")
                else:
                    cp = pl.op("vector", lambda e, dd=dd: e.tensor_scalar_mul(dd, pb[7][:, :], 1.0 / 64.0),
                               waits=xw, sem="cv")
            pbank67_free[w] = cp
            cps.append(cp)
        if kind == "w1c":
            wld_free[bi] = tkl
            W1T_ready[0] = tuple(cps)
        else:
            wt1_free[b3] = tkl
            if kind == "w1":
                w1col[job[1]] = tuple(cps)
            elif job[1] == KD - 1:
                w2grp[job[2]] = tuple(cps)

    def wq_flush():
        while pend_T:
            wq_back(pend_T.pop(0))

    pend_T = []
    for n in range(KD + 1):
        wq_job(n)

    xin_free = [None, None]
    xn_free = [None] * NRING
    one_free = {"xnb": None, "xqb": None, "qt": None, "qi": None, "attnb": None}
    xnfm_free = [None, None]
    xnb_wb = [None] * NTILE
    xq_wb = [None] * NTILE
    attn_wb = [None] * NTILE
    xn_ready = [None] * NTILE
    pbank_free = [None] * 8
    ln_prev = {}
    sp_prev = {}

    def phaseA_stage1(t):
        bi = t % 2
        ld = pl.op("sync", lambda e, t=t, bi=bi: e.dma_start(
            out=xin[bi][:, :], in_=x[t * P:(t + 1) * P, :]),
            waits=(xin_free[bi],), sem="dx%d" % bi, inc=16)
        s1 = pl.op("vector", lambda e, bi=bi: e.bn_stats(bn12[bi][:, 0:6], xin[bi][:, 0:512]),
                   waits=(ld, ln_prev.get("aggr%d" % bi)), sem="cv")
        s2 = pl.op("vector", lambda e, bi=bi: e.bn_stats(bn12[bi][:, 6:12], xin[bi][:, 512:1024]),
                   waits=(s1,), sem="cv")
        ag = pl.op("vector", lambda e, bi=bi: e.bn_aggr(bn2[bi][:, :], bn12[bi][:, :]),
                   waits=(s2, ln_prev.get("apply%d" % bi)), sem="cv")
        ln_prev["aggr%d" % bi] = ag
        l1 = pl.op("scalar", lambda e, bi=bi: e.activation(lnv[bi][:, :], bn2[bi][:, 1:2], AF.Ln, bias=ceps[:, :]),
                   waits=(ag, consts_tok, ln_prev.get("apply%d" % bi)), sem="cs")
        l2 = pl.op("scalar", lambda e, bi=bi: e.activation(rstd[bi][:, :], lnv[bi][:, :], AF.Exp, scale=cmhalf[:, :]),
                   waits=(l1,), sem="cs")
        return l2

    def phaseA_stage2(t, l2):
        bi = t % 2
        ri = t % NRING
        ap_ = pl.op("vector", lambda e, bi=bi, ri=ri: e.tensor_scalar(
            xn_tm[ri][:, :], xin[bi][:, :], bn2[bi][:, 0:1], rstd[bi][:, :], ALU.subtract, ALU.mult),
            waits=(l2, xn_free[ri]), sem="cv")
        ln_prev["apply%d" % bi] = ap_
        xin_free[bi] = ap_
        xn_ready[t] = ap_
        cb = pl.op("scalar", lambda e, ri=ri: e.activation(xnb[:, :], xn_tm[ri][:, :], AF.Copy),
                   waits=(ap_, one_free["xnb"]), sem="cs")
        wb1 = pl.op("gpsimd", lambda e, t=t: e.dma_start(
            out=s_xnb[t * P:(t + 1) * P, :], in_=xnb[:, :]),
            waits=(cb,), sem="dwbx", inc=16)
        one_free["xnb"] = wb1
        xnb_wb[t] = wb1
        q3 = None
        for h in range(2):
            q1 = pl.op("vector", lambda e, ri=ri, h=h: e.tensor_scalar(
                qt[:, :], xn_tm[ri][:, h * 512:(h + 1) * 512], 16.0, CMAG, ALU.mult, ALU.add),
                waits=(ap_, one_free["qt"]), sem="cv")
            q2 = pl.op("vector", lambda e: e.tensor_scalar(
                qi[:, :], qt[:, :], -CMAG, 127.0, ALU.add, ALU.min),
                waits=(q1, one_free["qi"]), sem="cv")
            one_free["qt"] = q2
            q3 = pl.op("vector", lambda e, h=h: e.tensor_scalar(
                xqb[:, h * 512:(h + 1) * 512], qi[:, :], -128.0, 1.0 / 16.0, ALU.max, ALU.mult),
                waits=(q2, one_free["xqb"] if h == 0 else None), sem="cv")
            one_free["qi"] = q3
        wb2 = pl.op("gpsimd", lambda e, t=t: e.dma_start(
            out=s_xq[t * P:(t + 1) * P, :], in_=xqb[:, :]),
            waits=(q3,), sem="dwbq", inc=16)
        one_free["xqb"] = wb2
        xq_wb[t] = wb2

    def phaseA_tile_tp(t):
        pbuf = (t // 2) % 2
        jj = t % 2
        tok = None
        for k in range(KD):
            tok = pl.op("sync", lambda e, k=k, t=t, jj=jj, pbuf=pbuf: e.dma_start(
                out=xn_fm[pbuf][:, k * 256 + jj * P:k * 256 + (jj + 1) * P],
                in_=s_xnb[t * P:(t + 1) * P, k * P:(k + 1) * P], transpose=True),
                waits=(xnb_wb[t],) + ((xnfm_free[pbuf],) if jj == 0 and k == 0 else ()),
                sem="txn%d" % (2 * pbuf + jj), inc=16)
        return tok

    def phaseA_delta(t, fm_tok):
        pbuf = (t // 2) % 2
        jj = t % 2
        par = t % 2
        bk0, bk1, bkc = (0, 1, 4) if par == 0 else (2, 3, 5)
        mw = [fm_tok, dsm_all, consts_tok,
              pbank_free[bk0], pbank_free[bk1], pbank_free[bkc]] + \
            list(W1T_ready[0] or ()) + [W23T_ready[0]]
        pl.op("tensor", lambda e, bkc=bkc: e.matmul(
            pb[bkc][:, 0:32], ones64[:, :], b23row[:, :], start=True, stop=False), waits=tuple(mw))
        pl.op("tensor", lambda e, bk0=bk0: e.matmul(
            pb[bk0][:, :], onesr[:, :], b1row[:, 0:512], start=True, stop=False))
        pl.op("tensor", lambda e, bk1=bk1: e.matmul(
            pb[bk1][:, :], onesr[:, :], b1row[:, 512:1024], start=True, stop=False))
        mm_tok = None
        for k in range(KD):
            last = k == KD - 1
            lo = k * 256 + jj * P
            pl.op("tensor", lambda e, k=k, lo=lo, bk0=bk0, pbuf=pbuf, last=last: e.matmul(
                pb[bk0][:, :], xn_fm[pbuf][:, lo:lo + P], W1T[:, k * D:k * D + 512],
                start=False, stop=last))
            pl.op("tensor", lambda e, k=k, lo=lo, bk1=bk1, pbuf=pbuf, last=last: e.matmul(
                pb[bk1][:, :], xn_fm[pbuf][:, lo:lo + P], W1T[:, k * D + 512:(k + 1) * D],
                start=False, stop=last))
            mm_tok = pl.op("tensor", lambda e, k=k, lo=lo, bkc=bkc, pbuf=pbuf, last=last: e.matmul(
                pb[bkc][:, 0:32], xn_fm[pbuf][:, lo:lo + P], W23T[:, k * 32:(k + 1) * 32],
                start=False, stop=last), sem="pe" if last else None)
        ebc = pl.op("scalar", lambda e, bkc=bkc: e.activation(evbc[:, :], pb[bkc][:, 0:32], AF.Copy),
                    waits=(mm_tok, sp_prev.get("evbc_free")), sem="cs")
        pr = pl.op("vector", lambda e: e.tensor_tensor(prodbc[:, :], evbc[:, 0:16], evbc[:, 16:32], ALU.mult),
                   waits=(ebc, sp_prev.get("prod_free")), sem="cv")
        sp_prev["evbc_free"] = pr
        rb = pl.op("vector", lambda e: e.reduce_sum(bcv[:, :], prodbc[:, :], AX.X),
                   waits=(pr, sp_prev.get("bc_free")), sem="cv")
        sp_prev["prod_free"] = rb
        at_tok = None
        slice_last = {}
        for s in range(2):
            bks = bk0 if s == 0 else bk1
            par2 = t % 2
            u = pl.op("scalar", lambda e, bks=bks, par=par2: e.activation(
                sp_u[par][:, :], pb[bks][:, :], AF.Abs),
                waits=(mm_tok, sp_prev.get("u_free%d" % par2)), sem="cs")
            v = pl.op("scalar", lambda e, par=par2: e.activation(
                sp_v[par][:, :], sp_u[par][:, :], AF.Exp, scale=-1.0),
                waits=(u, sp_prev.get("v_free%d" % par2)), sem="cs")
            w = pl.op("scalar", lambda e, par=par2: e.activation(
                sp_u[par][:, :], sp_v[par][:, :], AF.Ln, bias=1.0),
                waits=(v,), sem="cs")
            d = pl.op("vector", lambda e, bks=bks, par=par2: e.scalar_tensor_tensor(
                sp_v[par][:, :], pb[bks][:, :], 0.0, sp_u[par][:, :], ALU.max, ALU.add),
                waits=(w,), sem="cv")
            sp_prev["u_free%d" % par2] = d
            at_tok = pl.op("vector", lambda e, s=s, par=par2, ri=t % NRING: e.scalar_tensor_tensor(
                attnb[:, s * 512:(s + 1) * 512], sp_v[par][:, :], bcv[:, :],
                xn_tm[ri][:, s * 512:(s + 1) * 512], ALU.mult, ALU.mult),
                waits=(d, rb, xn_ready[t], one_free["attnb"]), sem="cv")
            sp_prev["v_free%d" % par2] = at_tok
            sp_prev["bc_free"] = at_tok
            slice_last[bks] = at_tok
        xn_free[t % NRING] = at_tok
        wb = pl.op("gpsimd", lambda e, t=t: e.dma_start(
            out=s_attn[t * P:(t + 1) * P, :], in_=attnb[:, :]),
            waits=(at_tok,), sem="dwba", inc=16)
        one_free["attnb"] = wb
        attn_wb[t] = wb
        pbank_free[bk0] = slice_last[bk0]
        pbank_free[bk1] = slice_last[bk1]
        pbank_free[bkc] = ebc
        return mm_tok

    njobs = len(wq_jobs)
    job_n = KD + 1
    per_pair = (njobs - job_n + NPAIR - 1) // NPAIR
    pending = None
    for p in range(NPAIR):
        l2a = phaseA_stage1(2 * p)
        l2b = phaseA_stage1(2 * p + 1)
        phaseA_stage2(2 * p, l2a)
        fm_a = phaseA_tile_tp(2 * p)
        phaseA_stage2(2 * p + 1, l2b)
        fm_b = phaseA_tile_tp(2 * p + 1)
        if pending is not None:
            pp, pfa, pfb = pending
            phaseA_delta(2 * pp, pfa)
            xnfm_free[pp % 2] = phaseA_delta(2 * pp + 1, pfb)
        pending = (p, fm_a, fm_b)
        for _ in range(per_pair):
            if job_n < njobs:
                wq_job(job_n)
                job_n += 1
    while job_n < njobs:
        wq_job(job_n)
        job_n += 1
    pp, pfa, pfb = pending
    phaseA_delta(2 * pp, pfa)
    last_delta_mm = phaseA_delta(2 * pp + 1, pfb)
    xnfm_free[pp % 2] = last_delta_mm
    last_delta[0] = last_delta_mm
    for n in range(njobs, njobs + 3):
        wq_job(n)
    defer_n = [njobs + 3]

    xqfm_free = [None, None]
    hq_free = [one_free["xnb"], one_free["xnb"]]
    epit_free = [one_free["xnb"]]
    epii_free = one_free["xqb"]
    xre_free = xn_free[0]
    atre_free = xn_free[1]
    t1_free = [xn_free[2], xn_free[2]]
    hbank_free = pbank_free

    def emit_xq_tp(g):
        gb = g % 2
        tok = None
        for k in range(KD):
            tok = pl.op("sync", lambda e, k=k, g=g, gb=gb: e.dma_start(
                out=xq_fm[gb][:, k * 256:(k + 1) * 256],
                in_=s_xq[g * 256:(g + 1) * 256, k * P:(k + 1) * P], transpose=True),
                waits=(xq_wb[2 * g + 1], xqfm_free[gb],
                       xin_free[gb] if g < 2 else None), sem="txq%d" % gb, inc=16)
        return tok

    xq_toks = [None] * NGRP
    xq_toks[0] = emit_xq_tp(0)
    for g in range(NGRP):
        gb = g % 2
        PO = [0, 1, 2, 3] if g % 2 == 0 else [4, 5, 6, 7]
        HB = [4, 5] if g % 2 == 0 else [0, 1]
        if g + 1 < NGRP:
            xq_toks[g + 1] = emit_xq_tp(g + 1)
        xq_tok = xq_toks[g]
        pw = [xq_tok, b2q_tok, b1q_tok, consts_tok] + \
             [pbank_free[b] for b in PO] + [pbank_free[b] for b in HB]
        for tt in range(2):
            for s in range(2):
                pl.op("tensor", lambda e, tt=tt, s=s, PO=PO: e.matmul(
                    pb[PO[2 * tt + s]][:, :], ones32[:, :], b2qrow[:, s * 512:(s + 1) * 512],
                    start=True, stop=False),
                    waits=tuple(pw) if (tt == 0 and s == 0) else ())
        lx0 = pl.op("sync", lambda e, t=2 * g: e.dma_start(
            out=xre[:, :], in_=x[t * P:(t + 1) * P, :]),
            waits=(xre_free,), sem="drx", inc=16)
        la0 = pl.op("sync", lambda e, t=2 * g: e.dma_start(
            out=atre[:, :], in_=s_attn[t * P:(t + 1) * P, :]),
            waits=(atre_free, attn_wb[2 * g]), sem="dra", inc=16)
        ffn2_bank_tok = [None] * 4
        last_xqfm_mm = None

        def emit_f2(j, hb, ep3, last_j):
            ftok = None
            for tt in range(2):
                for s in range(2):
                    need_sem = (tt == 1 and s == 1) or last_j
                    ftok = pl.op("tensor", lambda e, tt=tt, s=s, j=j, hb=hb, last_j=last_j, PO=PO: e.matmul(
                        pb[PO[2 * tt + s]][:, :], hq[hb][:, tt * P:(tt + 1) * P],
                        w2qT[:, j * D + s * 512:j * D + (s + 1) * 512],
                        start=False, stop=last_j),
                        waits=((ep3,) if (tt == 0 and s == 0) else ()) +
                              (w2grp[j // 8] or ()),
                        sem="pe" if need_sem else None)
                    if last_j:
                        ffn2_bank_tok[2 * tt + s] = ftok
            hq_free[hb] = ftok

        prev_f2 = None
        for j in range(KH):
            hb = j % 2
            bkh = HB[hb]
            h_done = None
            KORDER = (4, 5, 6, 7, 2, 3, 0, 1)
            for ki, k in enumerate(KORDER):
                last = ki == KD - 1
                h_done = pl.op("tensor", lambda e, k=k, j=j, bkh=bkh, gb=gb, last=last, first=ki == 0: e.matmul(
                    pb[bkh][:, 0:256], w1qT[:, k * HID + j * P:k * HID + (j + 1) * P],
                    xq_fm[gb][:, k * 256:(k + 1) * 256],
                    start=first, stop=last),
                    waits=((hbank_free[bkh], w1col[j][1]) if ki == 0 else
                           ((w1col[j][0],) if ki == 4 else ())),
                    sem="pe" if last else None)
            ep1 = pl.op("scalar", lambda e, j=j, bkh=bkh, hb=hb: e.activation(
                epi_t[hb][:, :], pb[bkh][:, 0:256], AF.Relu, bias=b1q16[:, j:j + 1], scale=c16[:, :]),
                waits=(h_done, b1q_tok, epit_free[0]), sem="cs")
            hbank_free[bkh] = ep1
            ep2 = pl.op("vector", lambda e, hb=hb: e.tensor_scalar(
                epi_i[:, :], epi_t[hb][:, :], CMAG, -CMAG, ALU.add, ALU.add),
                waits=(ep1, epii_free), sem="cv")
            epit_free[0] = ep2
            ep3 = pl.op("vector", lambda e, hb=hb: e.tensor_scalar(
                hq[hb][:, :], epi_i[:, :], 127.0, 1.0 / 16.0, ALU.min, ALU.mult),
                waits=(ep2, hq_free[hb]), sem="cv")
            epii_free = ep3
            pbank_free[bkh] = ep1
            if prev_f2 is not None:
                emit_f2(*prev_f2)
            prev_f2 = (j, hb, ep3, j == KH - 1)
            last_xqfm_mm = h_done
            if g == 0 and defer_n[0] < len(all_jobs) and j >= 1:
                wq_job(defer_n[0])
                defer_n[0] += 1
                if defer_n[0] == len(all_jobs):
                    wq_flush()
        emit_f2(*prev_f2)
        xqfm_free[gb] = last_xqfm_mm
        if g == 0:
            wq_flush()
            pbank_free[6] = pbank67_free[0]
            pbank_free[7] = pbank67_free[1]
        for tt in range(2):
            t = 2 * g + tt
            if tt == 0:
                lx, la = lx0, la0
            else:
                lx = pl.op("sync", lambda e, t=t: e.dma_start(
                    out=xre[:, :], in_=x[t * P:(t + 1) * P, :]),
                    waits=(xre_free,), sem="drx", inc=16)
                la = pl.op("sync", lambda e, t=t: e.dma_start(
                    out=atre[:, :], in_=s_attn[t * P:(t + 1) * P, :]),
                    waits=(atre_free, attn_wb[t]), sem="dra", inc=16)
            yhs = []
            for s in range(2):
                bk = 2 * tt + s
                t1 = pl.op("vector", lambda e, bk=bk, s=s, PO=PO: e.tensor_tensor(
                    t1b[s][:, :], pb[PO[bk]][:, :], atre[:, s * 512:(s + 1) * 512], ALU.add),
                    waits=(ffn2_bank_tok[bk], la, t1_free[s]), sem="cv")
                pbank_free[PO[bk]] = t1
                yh = pl.op("vector", lambda e, s=s: e.tensor_tensor(
                    t1b[s][:, :], t1b[s][:, :],
                    xre[:, s * 512:(s + 1) * 512], ALU.add),
                    waits=(t1, lx), sem="cv")
                yhs.append(yh)
                st_ = pl.op("gpsimd", lambda e, t=t, s=s: e.dma_start(
                    out=y[t * P:(t + 1) * P, s * 512:(s + 1) * 512], in_=t1b[s][:, :]),
                    waits=(yh,), sem="dst%d" % s, inc=16)
                t1_free[s] = st_
            xre_free = yhs[-1]
            atre_free = yhs[-1]

    pl.raw("gpsimd", lambda e, S: e.wait_ge(S["dst0"], 16 * NTILE))
    pl.raw("gpsimd", lambda e, S: e.wait_ge(S["dst1"], 16 * NTILE))

    with contextlib.ExitStack() as ctx2:
        S = {}
        for s in SEMS:
            S[s] = ctx2.enter_context(nc.semaphore(s))
        block = ctx2.enter_context(nc.Block())
        block.sync(lambda e: [f(e, S) for f in pl.ops["sync"]])
        block.scalar(lambda e: [f(e, S) for f in pl.ops["scalar"]])
        block.vector(lambda e: [f(e, S) for f in pl.ops["vector"]])
        block.tensor(lambda e: [f(e, S) for f in pl.ops["tensor"]])
        block.gpsimd(lambda e: [f(e, S) for f in pl.ops["gpsimd"]])
    stack.close()
    return nc


_NC_CACHE = [None]


def kernel(L0_attn_x, L0_attn_W1, L0_attn_b1, L0_attn_W2, L0_attn_b2,
           L0_attn_W3, L0_attn_b3, L0_attn_A,
           L0_ffn_w1, L0_ffn_b1, L0_ffn_w2, L0_ffn_b2):
    xf = np.ascontiguousarray(np.asarray(L0_attn_x, dtype=np.float32).reshape(NCORES * T, D))
    com = dict(
        W1=np.ascontiguousarray(np.asarray(L0_attn_W1, dtype=np.float32)),
        b1=np.ascontiguousarray(np.asarray(L0_attn_b1, dtype=np.float32).reshape(1, D)),
        W2=np.ascontiguousarray(np.asarray(L0_attn_W2, dtype=np.float32)),
        b2=np.ascontiguousarray(np.asarray(L0_attn_b2, dtype=np.float32).reshape(1, NST)),
        W3=np.ascontiguousarray(np.asarray(L0_attn_W3, dtype=np.float32)),
        b3=np.ascontiguousarray(np.asarray(L0_attn_b3, dtype=np.float32).reshape(1, NST)),
        fw1=np.ascontiguousarray(np.asarray(L0_ffn_w1, dtype=np.float32)),
        fb1=np.ascontiguousarray(np.asarray(L0_ffn_b1, dtype=np.float32).reshape(HID, 1)),
        fw2=np.ascontiguousarray(np.asarray(L0_ffn_w2, dtype=np.float32)),
        fb2=np.ascontiguousarray(np.asarray(L0_ffn_b2, dtype=np.float32).reshape(1, D)),
    )
    if _NC_CACHE[0] is None:
        _NC_CACHE[0] = build_nc()
    nc = _NC_CACHE[0]
    in_maps = [dict(com, x=xf[c * T:(c + 1) * T]) for c in range(NCORES)]
    res = run_bass_kernel_spmd(nc, in_maps, core_ids=list(range(NCORES)))
    out = np.concatenate([np.asarray(res.results[c]["y"]) for c in range(NCORES)], axis=0)
    return out.reshape(4, 4096, D).astype(np.float32, copy=False)


if __name__ == "__main__":
    nc = build_nc()
    print("build ok")
